# version 16
# baseline (speedup 1.0000x reference)
import math
import numpy as np
import ml_dtypes

import concourse.bass as bass
import concourse.mybir as mybir
from concourse.bass_utils import run_bass_kernel_spmd

F32 = mybir.dt.float32
BF16 = mybir.dt.bfloat16
FP8 = mybir.dt.float8e4
I32 = mybir.dt.int32
ACT = mybir.ActivationFunctionType
ALU = mybir.AluOpType

NCORE = 8
D = 2048
S = 1024
SF = 8192
I_FF = 6144
KC = D // 128
IC = I_FF // 128
OC = D // 128
NH, NKV, HD = 16, 8, 128
NHEADS = NH + NKV
THETA = 10000.0
MAGIC = float(1.5 * 2 ** 23)
INV2PI = float(1.0 / (2.0 * math.pi))
TWOPI = float(2.0 * math.pi)

H0, H1 = 0, 512


def build_nc(debug_sync=False):
    nc = bass.Bass("TRN2", target_bir_lowering=False, num_devices=NCORE,
                   detect_race_conditions=debug_sync)

    xfm = nc.dram_tensor("xfm", [D, S], F32, kind="ExternalInput").ap()
    wqT = nc.dram_tensor("wqT", [D, D], F32, kind="ExternalInput").ap()
    wkT = nc.dram_tensor("wkT", [D, NKV * HD], F32, kind="ExternalInput").ap()
    wvT = nc.dram_tensor("wvT", [D, NKV * HD], F32, kind="ExternalInput").ap()
    woT = nc.dram_tensor("woT", [D, D], F32, kind="ExternalInput").ap()
    gw = nc.dram_tensor("gw", [D, I_FF], F32, kind="ExternalInput").ap()
    uw = nc.dram_tensor("uw", [D, I_FF], F32, kind="ExternalInput").ap()
    dw = nc.dram_tensor("dw", [I_FF, D], F32, kind="ExternalInput").ap()
    cosz = nc.dram_tensor("cosz", [128, S], F32, kind="ExternalInput").ap()
    sinp = nc.dram_tensor("sinp", [128, S], F32, kind="ExternalInput").ap()
    rtdup = nc.dram_tensor("rtdup", [128, 128], F32, kind="ExternalInput").ap()
    r2tdup = nc.dram_tensor("r2tdup", [128, 128], F32, kind="ExternalInput").ap()
    bqv = nc.dram_tensor("bqv", [128, NH], F32, kind="ExternalInput").ap()
    bkv = nc.dram_tensor("bkv", [128, NKV], F32, kind="ExternalInput").ap()
    bvv = nc.dram_tensor("bvv", [128, NKV], F32, kind="ExternalInput").ap()
    bov = nc.dram_tensor("bov", [128, OC], F32, kind="ExternalInput").ap()
    gsv = nc.dram_tensor("gsv", [128, IC], F32, kind="ExternalInput").ap()
    usv = nc.dram_tensor("usv", [128, IC], F32, kind="ExternalInput").ap()
    dsv = nc.dram_tensor("dsv", [128, OC], F32, kind="ExternalInput").ap()
    cmean = nc.dram_tensor("cmean", [128, 128], F32, kind="ExternalInput").ap()
    cone = nc.dram_tensor("cone", [128, 1], F32, kind="ExternalInput").ap()
    caf = nc.dram_tensor("caf", [128, 128], F32, kind="ExternalInput").ap()
    crows = nc.dram_tensor("crows", [1, 512], F32, kind="ExternalInput").ap()
    cshift = nc.dram_tensor("cshift", [1, 128], F32, kind="ExternalInput").ap()
    ceps = nc.dram_tensor("ceps", [1, 128], F32, kind="ExternalInput").ap()

    ofm = nc.dram_tensor("ofm", [OC, 128, S], F32, kind="ExternalOutput").ap()

    x1s = nc.dram_tensor("x1s", [KC, 128, S], F32, kind="Internal").ap()
    x2s = nc.dram_tensor("x2s", [KC, 128, S], F32, kind="Internal").ap()
    ag_in = nc.dram_tensor("ag_in", [NH, 128, S], BF16, kind="Internal").ap()
    ag_outA = nc.dram_tensor("ag_outA", [NCORE, 8, 128, S], BF16,
                             kind="Internal").ap()
    ag_outB = nc.dram_tensor("ag_outB", [NCORE, 8, 128, S], BF16,
                             kind="Internal").ap()

    from contextlib import ExitStack
    ctx = ExitStack()
    e = ctx.enter_context

    xb = e(nc.sbuf_tensor("xb", [128, KC * S], BF16))
    big = e(nc.sbuf_tensor("big", [128, 48 * S], BF16))
    cosT = e(nc.sbuf_tensor("cosT", [128, S], BF16))
    sinP = e(nc.sbuf_tensor("sinP", [128, S], BF16))
    xs = [e(nc.sbuf_tensor(f"xs{i}", [128, S], F32)) for i in range(2)]
    stg = [e(nc.sbuf_tensor(f"stg{i}", [128, S], F32)) for i in range(2)]
    wbufA = [e(nc.sbuf_tensor(f"wbA{i}", [128, KC * 128], FP8)) for i in range(2)]
    wbufB = [e(nc.sbuf_tensor(f"wbB{i}", [128, KC * 128], FP8)) for i in range(2)]
    wbufD = [e(nc.sbuf_tensor(f"wbD{i}", [128, IC * 128], FP8)) for i in range(2)]
    wbufQ = [e(nc.sbuf_tensor(f"wbQ{i}", [128, KC * 128], BF16)) for i in range(2)]
    scrF = [e(nc.sbuf_tensor(f"scrF{i}", [128, S], F32)) for i in range(5)]
    scrB = [e(nc.sbuf_tensor(f"scrB{i}", [128, S], BF16)) for i in range(4)]
    biases = e(nc.sbuf_tensor("biases", [128, 64], F32))
    dsh = e(nc.sbuf_tensor("dsh", [128, OC], F32))
    consts = e(nc.sbuf_tensor("consts", [128, 258], BF16))
    rt_bf = e(nc.sbuf_tensor("rt_bf", [128, 2 * 128], BF16))
    rows_bf = e(nc.sbuf_tensor("rows_bf", [1, 512 + 2 * 128], BF16))
    onesf = e(nc.sbuf_tensor("onesf", [1, 128], F32))

    psA = e(nc.psum_tensor("psA", [128, S], F32))
    psB = e(nc.psum_tensor("psB", [128, S], F32))
    psC = e(nc.psum_tensor("psC", [128, S], F32))
    psD = e(nc.psum_tensor("psD", [128, S], F32))
    PS = [psA, psB, psC, psD]

    gsus = e(nc.sbuf_tensor("gsus", [128, 2 * IC], F32))

    sem_names = ["t", "v", "s", "cc", "pre", "preg",
                 "wg0", "wg1", "wu0", "wu1", "wd0", "wd1", "wq0", "wq1",
                 "xs0", "xs1", "st0", "st1", "agw", "agr", "nsg0", "nsg1", "nsc0", "nsc1", "g"]
    SEM = {n: e(nc.semaphore(n)) for n in sem_names}
    cnt = {n: 0 for n in sem_names}

    ops = {k: [] for k in ("tensor", "vector", "scalar", "sync", "gpsimd")}
    water = {k: {} for k in ops}

    def emit(eng, fn):
        ops[eng].append(fn)

    def wait(eng, sem, val):
        if val <= 0:
            return
        if water[eng].get(sem, -1) == val:
            return
        water[eng][sem] = val
        emit(eng, lambda E, s=sem, v=val: E.wait_ge(SEM[s], v))

    def bump(sem, by):
        cnt[sem] += by
        return cnt[sem]

    marks = {}

    def res_wait(eng, res):
        for sem, val in marks.get(res, ()):
            wait(eng, sem, val)

    def res_set(res, sem, val):
        marks[res] = [(sem, val)]

    def res_add(res, sem, val):
        marks.setdefault(res, []).append((sem, val))

    def dma(eng, qsem, out_ap, in_ap):
        v = bump(qsem, 16)
        emit(eng, lambda E, o=out_ap, i=in_ap, s=qsem, vv=v:
             E.dma_start(out=o, in_=i).then_inc(SEM[s], 16))
        return v

    def act(out_ap, in_ap, func, bias=0.0, scale=1.0, inc=True):
        v = bump("s", 1) if inc else None
        def f(E, o=out_ap, i=in_ap, fn=func, b=bias, sc=scale, vv=v):
            ins = E.activation(out=o, in_=i, func=fn, bias=b, scale=sc)
            if vv is not None:
                ins.then_inc(SEM["s"], 1)
        emit("scalar", f)
        if debug_sync and v is not None:
            emit("scalar", lambda E, vv=v: E.wait_ge(SEM["s"], vv))
        return cnt["s"]

    def vop(name, *args, inc=True, **kw):
        v = bump("v", 1) if inc else None
        def f(E, n=name, a=args, k=kw, vv=v):
            ins = getattr(E, n)(*a, **k)
            if vv is not None:
                ins.then_inc(SEM["v"], 1)
        emit("vector", f)
        if debug_sync and v is not None:
            emit("vector", lambda E, vv=v: E.wait_ge(SEM["v"], vv))
        return cnt["v"]

    def gop(name, *args, inc=True, **kw):
        v = bump("g", 1) if inc else None
        def f(E, n=name, a=args, k=kw, vv=v):
            ins = getattr(E, n)(*a, **k)
            if vv is not None:
                ins.then_inc(SEM["g"], 1)
        emit("gpsimd", f)
        if debug_sync and v is not None:
            emit("gpsimd", lambda E, vv=v: E.wait_ge(SEM["g"], vv))
        return cnt["g"]

    def mm(out_ap, lhsT, rhs, start, stop, inc=False):
        v = bump("t", 1) if inc else None
        def f(E, o=out_ap, l=lhsT, r=rhs, st=start, sp=stop, vv=v):
            ins = E.matmul(o, l, r, start=st, stop=sp)
            if vv is not None:
                ins.then_inc(SEM["t"], 1)
        emit("tensor", f)
        return cnt["t"]

    pre_loads = [
        (biases[:, 0:NH], bqv), (biases[:, 16:16 + NKV], bkv),
        (biases[:, 24:24 + NKV], bvv), (biases[:, 32:32 + OC], bov),
        (biases[:, 48:48 + OC], dsv),
        (gsus[:, 0:IC], gsv), (gsus[:, IC:2 * IC], usv),
        (onesf[:, 0:128], crows[0:1, 0:128]),
    ]
    for o, i in pre_loads:
        dma("sync", "pre", o, i)
    cast_loads = [
        (cosT[:], cosz), (sinP[:], sinp),
        (consts[:, 0:128], cmean), (consts[:, 128:256], caf),
        (consts[:, 256:257], cone),
        (rt_bf[:, 0:128], rtdup), (rt_bf[:, 128:256], r2tdup),
        (rows_bf[:, 0:512], crows), (rows_bf[:, 512:512 + 128], cshift),
        (rows_bf[:, 512 + 128:512 + 256], ceps),
    ]
    for o, i in cast_loads:
        dma("gpsimd", "preg", o, i)
    npre = cnt["pre"]
    npreg = cnt["preg"]
    wait("scalar", "pre", npre)
    wait("scalar", "preg", npreg)
    emit("scalar", lambda E: E.preload_act_table(ACT.Silu))
    act(dsh[:], biases[:, 48:48 + OC], ACT.Copy, scale=0.5)
    wait("tensor", "pre", npre)
    wait("tensor", "preg", npreg)
    wait("vector", "pre", npre)
    wait("vector", "preg", npreg)

    C_MEAN = consts[:, 0:128]
    C_AF = consts[:, 128:256]
    C_ONE = consts[:, 256:258]
    RT_BF = rt_bf[:, 0:128]
    R2T_BF = rt_bf[:, 128:256]
    ROW1 = rows_bf[:, 0:512]
    SHIFTC = rows_bf[0:1, 512:512 + 128]
    EPSC = rows_bf[0:1, 512 + 128:512 + 256]
    ONESCOL_F = onesf[0:1, :]

    def newton_rsqrt(out_t, val_ap, u_t, iters=3):
        oi = out_t[:].bitcast(I32)
        vop("tensor_scalar", oi, val_ap.bitcast(I32), 1, None, ALU.arith_shift_right)
        vop("tensor_scalar", oi, oi, -1, 0x5F3759DF, ALU.mult, ALU.add)
        for _ in range(iters):
            vop("tensor_mul", u_t[:], out_t[:], out_t[:])
            vop("scalar_tensor_tensor", u_t[:], val_ap, -0.5, u_t[:], ALU.mult, ALU.mult)
            vop("scalar_tensor_tensor", out_t[:], u_t[:], 1.5, out_t[:], ALU.add, ALU.mult)

    def ln_chunk_ew(k, src, src_is_sbuf=False, src_marks=()):
        if src_is_sbuf:
            for sem, val in src_marks:
                wait("scalar", sem, val)
            xsrc = src
        else:
            sl = k % 2
            res_wait("sync", f"xs{sl}")
            lv = dma("sync", f"xs{sl}", xs[sl][:], src)
            wait("scalar", f"xs{sl}", lv)
            xsrc = xs[sl][:]
        xbk = xb[:, k * S:(k + 1) * S]
        sv = act(xbk, xsrc, ACT.Copy)
        if not src_is_sbuf:
            res_set(f"xs{k % 2}", "s", sv)
        sq = scrB[2 + (k % 2)]
        res_wait("vector", f"SB{2 + (k % 2)}")
        wait("vector", "s", sv)
        vv = vop("tensor_mul", sq[:], xbk, xbk)
        return sv, vv

    def ln_stats_mm(k, sv, vv):
        xbk = xb[:, k * S:(k + 1) * S]
        sq = scrB[2 + (k % 2)]
        wait("tensor", "s", sv)
        wait("tensor", "v", vv)
        if k == 0:
            res_wait("tensor", "PS2")
            res_wait("tensor", "PS3")
        for h0 in (H0, H1):
            mm(psC[:, h0:h0 + 512], C_MEAN, xbk[:, h0:h0 + 512],
               start=(k == 0), stop=(k == KC - 1))
        for h0 in (H0, H1):
            last = h0 == H1
            mm(psD[:, h0:h0 + 512], C_MEAN, sq[:, h0:h0 + 512],
               start=(k == 0), stop=(k == KC - 1), inc=last)
        res_set(f"SB{2 + (k % 2)}", "t", cnt["t"])

    def ln_chunk(k, src, src_is_sbuf=False, src_marks=()):
        sv, vv = ln_chunk_ew(k, src, src_is_sbuf, src_marks)
        ln_stats_mm(k, sv, vv)
        return sv

    def ln_tail():
        wait("vector", "t", cnt["t"])
        mean_t, rstd_t = scrF[3], scrF[4]
        vop("tensor_copy", mean_t[:], psC[:])
        var_t, u_t = scrF[0], scrF[1]
        vop("scalar_tensor_tensor", var_t[:], mean_t[:], -1.0, mean_t[:],
            ALU.mult, ALU.mult)
        vop("tensor_add", var_t[:], var_t[:], psD[:])
        vv = vop("tensor_scalar", var_t[:], var_t[:], 1e-5, None, ALU.add)
        res_set("PS2", "v", vv)
        res_set("PS3", "v", vv)
        newton_rsqrt(rstd_t, var_t[:], u_t)
        for k in range(KC):
            xbk = xb[:, k * S:(k + 1) * S]
            t0 = scrF[2]
            vop("scalar_tensor_tensor", t0[:], xbk, 1.0, mean_t[:],
                ALU.mult, ALU.subtract)
            vop("tensor_mul", xbk, t0[:], rstd_t[:])
        return bump("v", 0)

    def ln_phase(src_kind, src_ap):
        for k in range(KC):
            if src_kind == "xfm":
                chunk = src_ap[k * 128:(k + 1) * 128, :]
            else:
                chunk = src_ap[k]
            ln_chunk(k, chunk)
        return ln_tail()

    def ffn_gateup(xbc_ready_v):
        wait("tensor", "v", xbc_ready_v)
        for i in range(IC):
            sl = i % 2
            res_wait("gpsimd", f"gbuf{sl}")
            gv = dma("gpsimd", f"wg{sl}",
                     wbufA[sl][:].rearrange("p (c i) -> p c i", i=128),
                     gw[:, i * 128:(i + 1) * 128].rearrange("(c p) i -> p c i", p=128))
            res_wait("gpsimd", f"ubuf{sl}")
            uv = dma("gpsimd", f"wu{sl}",
                     wbufB[sl][:].rearrange("p (c i) -> p c i", i=128),
                     uw[:, i * 128:(i + 1) * 128].rearrange("(c p) i -> p c i", p=128))
            wait("tensor", f"wg{sl}", gv)
            wait("tensor", f"wu{sl}", uv)
            psG = PS[i % 2]
            psU = PS[2 + (i % 2)]
            res_wait("tensor", f"PS{i % 2}")
            res_wait("tensor", f"PS{2 + (i % 2)}")
            for h0 in (H0, H1):
                for k in range(KC):
                    mm(psG[:, h0:h0 + 512], wbufA[sl][:, k * 128:(k + 1) * 128],
                       xb[:, k * S + h0:k * S + h0 + 512],
                       start=(k == 0), stop=(k == KC - 1))
            for h0 in (H0, H1):
                for k in range(KC):
                    last = (h0 == H1) and (k == KC - 1)
                    mm(psU[:, h0:h0 + 512], wbufB[sl][:, k * 128:(k + 1) * 128],
                       xb[:, k * S + h0:k * S + h0 + 512],
                       start=(k == 0), stop=(k == KC - 1), inc=last)
            tmark = cnt["t"]
            res_set(f"gbuf{sl}", "t", tmark)
            res_set(f"ubuf{sl}", "t", tmark)
            wait("scalar", "t", tmark)
            gact, ups = scrB[0], scrB[1]
            res_wait("scalar", "gact")
            res_wait("scalar", "ups")
            act(gact[:], psG[:], ACT.Silu, scale=gsus[:, i:i + 1])
            sv = act(ups[:], psU[:], ACT.Copy, scale=gsus[:, IC + i:IC + i + 1])
            res_set(f"PS{i % 2}", "s", sv)
            res_set(f"PS{2 + (i % 2)}", "s", sv)
            wait("vector", "s", sv)
            fv = vop("tensor_mul", big[:, i * S:(i + 1) * S], gact[:], ups[:])
            res_set("gact", "v", fv)
            res_set("ups", "v", fv)
        return cnt["v"]

    def ffn_down(f_ready_v, src_kind, src_ap, dst_kind, dst_ap, ln_follow):
        LN_PEND = []
        wait("tensor", "v", f_ready_v)
        for o in range(OC):
            sl = o % 2
            res_wait("gpsimd", f"dbuf{sl}")
            dv = dma("gpsimd", f"wd{sl}",
                     wbufD[sl][:].rearrange("p (c o) -> p c o", o=128),
                     dw[:, o * 128:(o + 1) * 128].rearrange("(c p) o -> p c o", p=128))
            res_wait("sync", f"xs{sl}")
            if src_kind == "xfm":
                chunk = src_ap[o * 128:(o + 1) * 128, :]
            else:
                chunk = src_ap[o]
            lv = dma("sync", f"xs{sl}", xs[sl][:], chunk)
            wait("tensor", f"wd{sl}", dv)
            psO = PS[o % 2]
            res_wait("tensor", f"PS{o % 2}")
            for h0 in (H0, H1):
                for i in range(IC):
                    last = (h0 == H1) and (i == IC - 1)
                    mm(psO[:, h0:h0 + 512], wbufD[sl][:, i * 128:(i + 1) * 128],
                       big[:, i * S + h0:i * S + h0 + 512],
                       start=(i == 0), stop=(i == IC - 1), inc=last)
            tmark = cnt["t"]
            res_set(f"dbuf{sl}", "t", tmark)
            wait("vector", "t", tmark)
            wait("vector", f"xs{sl}", lv)
            res_wait("vector", f"stg{sl}")
            vv = vop("scalar_tensor_tensor", stg[sl][:], psO[:],
                     dsh[:, o:o + 1], xs[sl][:], ALU.mult, ALU.add)
            res_set(f"PS{o % 2}", "v", vv)
            res_set(f"xs{sl}", "v", vv)
            wait("sync", "v", vv)
            stv = dma("sync", f"st{sl}", dst_ap[o], stg[sl][:])
            res_set(f"stg{sl}", f"st{sl}", stv)
            if ln_follow:
                sv2, vv2 = ln_chunk_ew(o, stg[sl][:], src_is_sbuf=True,
                                       src_marks=[("v", vv)])
                res_add(f"stg{sl}", "s", sv2)
                LN_PEND.append((o, sv2, vv2))
                if len(LN_PEND) > 1:
                    ln_stats_mm(*LN_PEND.pop(0))
        if ln_follow:
            while LN_PEND:
                ln_stats_mm(*LN_PEND.pop(0))
    def qkv_phase(xbc_ready_v):
        wait("tensor", "v", xbc_ready_v)
        plan = []
        for h in range(NH):
            plan.append((wqT, h, biases[:, h:h + 1]))
        for j in range(NKV):
            plan.append((wkT, j, biases[:, 16 + j:16 + j + 1]))
        for j in range(NKV):
            plan.append((wvT, j, biases[:, 24 + j:24 + j + 1]))
        for idx, (wt, col, bias_col) in enumerate(plan):
            sl = idx % 2
            res_wait("gpsimd", f"qbuf{sl}")
            qv = dma("gpsimd", f"wq{sl}",
                     wbufQ[sl][:].rearrange("p (c o) -> p c o", o=128),
                     wt[:, col * 128:(col + 1) * 128].rearrange("(c p) o -> p c o", p=128))
            wait("tensor", f"wq{sl}", qv)
            psQ = PS[idx % 2]
            res_wait("tensor", f"PS{idx % 2}")
            for h0 in (H0, H1):
                for k in range(KC):
                    last = (h0 == H1) and (k == KC - 1)
                    mm(psQ[:, h0:h0 + 512], wbufQ[sl][:, k * 128:(k + 1) * 128],
                       xb[:, k * S + h0:k * S + h0 + 512],
                       start=(k == 0), stop=(k == KC - 1), inc=last)
            tmark = cnt["t"]
            res_set(f"qbuf{sl}", "t", tmark)
            wait("scalar", "t", tmark)
            sv = act(big[:, idx * S:(idx + 1) * S], psQ[:], ACT.Identity,
                     bias=bias_col)
            res_set(f"PS{idx % 2}", "s", sv)

    def feature_phase():
        nsq24 = scrF[3]
        rowb = [scrF[4][0:1, :], scrF[0][0:1, :]]
        wait("scalar", "s", cnt["s"])
        for h in range(NHEADS):
            qh = big[:, h * S:(h + 1) * S]
            sq = scrB[2 + (h % 2)]
            res_wait("scalar", f"SB{2 + (h % 2)}")
            sv = act(sq[:], qh, ACT.Square)
            psN = PS[h % 4]
            res_wait("tensor", f"PS{h % 4}")
            wait("tensor", "s", sv)
            for h0 in (H0, H1):
                mm(psN[0:1, h0:h0 + 512], C_ONE[:, 0:1], sq[:, h0:h0 + 512],
                   start=True, stop=False)
                last = h0 == H1
                mm(psN[0:1, h0:h0 + 512], EPSC[0:1, 0:1], ROW1[0:1, 0:512],
                   start=False, stop=True, inc=last)
            tnv = cnt["t"]
            res_set(f"SB{2 + (h % 2)}", "t", tnv)
            wait("scalar", "t", tnv)
            res_wait("scalar", f"rowb{h % 2}")
            sv = act(rowb[h % 2], psN[0:1, :], ACT.Copy)
            res_set(f"PS{h % 4}", "s", sv)
            wait("sync", "s", sv)
            gv = dma("sync", f"nsg{h % 2}", nsq24[h:h + 1, :], rowb[h % 2])
            res_set(f"rowb{h % 2}", f"nsg{h % 2}", gv)
        wait("vector", "nsg0", cnt["nsg0"])
        wait("vector", "nsg1", cnt["nsg1"])
        y24 = scrF[2][0:24, :]
        u24 = scrF[0][0:24, :]
        val = nsq24[0:24, :]
        yi = y24.bitcast(I32)
        vop("tensor_scalar", yi, val.bitcast(I32), 1, None, ALU.arith_shift_right)
        vop("tensor_scalar", yi, yi, -1, 0x5F3759DF, ALU.mult, ALU.add)
        for it in range(3):
            vop("tensor_mul", u24, y24, y24)
            vop("scalar_tensor_tensor", u24, val, -0.5, u24, ALU.mult, ALU.mult)
            dst = val if it == 2 else y24
            nv = vop("scalar_tensor_tensor", dst, u24, 1.5, y24, ALU.add, ALU.mult)
        wait("sync", "st0", cnt["st0"])
        wait("sync", "st1", cnt["st1"])
        ld_ones32 = dma("sync", "pre", stg[0][32:33, 0:128], crows[0:1, 0:128])
        rowb2s = [(scrF[4][0:1, :], ONESCOL_F, "nsc0", "rowb2A"),
                  (stg[1][32:33, :], stg[0][32:33, 0:128], "nsc1", "rowb2B")]
        rk_bufs = [(scrF[2], "rkb0"), (xs[0], "xs0")]
        rr_bufs = [(scrF[1], "rrb0"), (xs[1], "xs1")]
        fst = {}

        def fm_stage1(h):
            rowb2, onesc, nscs, rkey = rowb2s[h % 2]
            res_wait("sync", rkey)
            wait("sync", "v", nv)
            wait("sync", "pre", ld_ones32)
            cv = dma("sync", nscs, rowb2, nsq24[h:h + 1, :])
            psN = PS[(h % 2) * 2]
            res_wait("tensor", f"PS{(h % 2) * 2}")
            wait("tensor", "pre", ld_ones32)
            wait("tensor", nscs, cv)
            for h0 in (H0, H1):
                last = h0 == H1
                mm(psN[:, h0:h0 + 512], onesc, rowb2[0:1, h0:h0 + 512],
                   start=True, stop=True, inc=last)
            fst[h] = {"bv_t": cnt["t"]}
            res_set(rkey, "t", fst[h]["bv_t"])

        def fm_stage2(h):
            qh = big[:, h * S:(h + 1) * S]
            psN = PS[(h % 2) * 2]
            wait("vector", "t", fst[h]["bv_t"])
            tq = scrF[0]
            tqv = vop("tensor_mul", tq[:], qh, psN[:])
            res_set(f"PS{(h % 2) * 2}", "v", tqv)
            if h >= 1:
                wait("vector", "t", fst[h - 1]["tpv"])
            t1, t2 = scrB[0], scrB[1]
            vop("tensor_mul", t1[:], tq[:], cosT[:])
            fst[h]["nv2"] = vop("tensor_mul", t2[:], tq[:], sinP[:])

        def fm_stage3(h):
            psP = PS[(h % 2) * 2 + 1]
            t1, t2 = scrB[0], scrB[1]
            res_wait("tensor", f"PS{(h % 2) * 2 + 1}")
            wait("tensor", "v", fst[h]["nv2"])
            for h0 in (H0, H1):
                mm(psP[:, h0:h0 + 512], RT_BF, t1[:, h0:h0 + 512],
                   start=True, stop=False)
                mm(psP[:, h0:h0 + 512], R2T_BF, t2[:, h0:h0 + 512],
                   start=False, stop=False)
                last = h0 == H1
                mm(psP[:, h0:h0 + 512], SHIFTC, ROW1[0:1, 0:512],
                   start=False, stop=True, inc=last)
            fst[h]["tpv"] = cnt["t"]
            rkb, rk_res = rk_bufs[h % 2]
            res_wait("scalar", rk_res)
            wait("scalar", "t", fst[h]["tpv"])
            act(rkb[:], psP[:], ACT.Copy, scale=INV2PI, bias=MAGIC)
            fst[h]["rk2v"] = act(rkb[:], rkb[:], ACT.Copy, bias=-MAGIC)

        def fm_stage4(h):
            psP = PS[(h % 2) * 2 + 1]
            rkb, rk_res = rk_bufs[h % 2]
            rrb, rr_res = rr_bufs[h % 2]
            res_wait("vector", rr_res)
            wait("vector", "s", fst[h]["rk2v"])
            rv = vop("scalar_tensor_tensor", rrb[:], rkb[:], -TWOPI, psP[:],
                     ALU.mult, ALU.add)
            res_set(f"PS{(h % 2) * 2 + 1}", "v", rv)
            res_set(rk_res, "v", rv)
            wait("scalar", "v", rv)
            sv = act(big[:, h * S:(h + 1) * S], rrb[:], ACT.Sin)
            res_set(rr_res, "s", sv)
            del fst[h]

        for h in range(NHEADS):
            fm_stage1(h)
            if h >= 1:
                fm_stage2(h - 1)
                fm_stage3(h - 1)
            if h >= 2:
                fm_stage4(h - 2)
        fm_stage2(NHEADS - 1)
        fm_stage3(NHEADS - 1)
        fm_stage4(NHEADS - 2)
        fm_stage4(NHEADS - 1)

    def a_out_phase(h_lo, h_hi, vec_heads=()):
        OUT_OFF = 32 * S
        wait("gpsimd", "s", cnt["s"])
        for h in range(h_lo, h_hi):
            qp = big[:, h * S:(h + 1) * S]
            outacc = scrF[0]
            for j in range(NKV):
                kp = big[:, (NH + j) * S:(NH + j + 1) * S]
                vj = big[:, (NH + NKV + j) * S:(NH + NKV + j + 1) * S]
                prod = scrB[2 + (j % 2)]
                if h in vec_heads:
                    res_wait("vector", f"SB{2 + (j % 2)}")
                    pv = vop("tensor_mul", prod[:], qp, kp)
                    psem = "v"
                else:
                    res_wait("gpsimd", f"SB{2 + (j % 2)}")
                    pv = gop("tensor_mul", prod[:], qp, kp)
                    psem = "g"
                psX = PS[j % 2]
                res_wait("tensor", f"PS{j % 2}")
                wait("tensor", psem, pv)
                for h0 in (H0, H1):
                    last = h0 == H1
                    mm(psX[:, h0:h0 + 512], C_AF, prod[:, h0:h0 + 512],
                       start=True, stop=True, inc=last)
                tv = cnt["t"]
                res_set(f"SB{2 + (j % 2)}", "t", tv)
                wait("vector", "t", tv)
                if j == 0:
                    res_wait("vector", "outacc")
                    av = vop("tensor_mul", outacc[:], psX[:], vj)
                else:
                    tmp = scrB[0]
                    vop("tensor_mul", tmp[:], psX[:], vj)
                    av = vop("tensor_add", outacc[:], outacc[:], tmp[:])
                res_set(f"PS{j % 2}", "v", av)
            wait("scalar", "v", cnt["v"])
            sv = act(big[:, OUT_OFF + h * S:OUT_OFF + (h + 1) * S], outacc[:],
                     ACT.Copy)
            res_set("outacc", "s", sv)

    def exchange_half(half, ag_out_t):
        OUT_OFF = 32 * S + half * 8 * S
        wait("sync", "s", cnt["s"])
        wait("sync", "agw", cnt["agw"])
        agv = dma("sync", "agw", ag_in[half * 8:(half + 1) * 8].transpose([1, 0, 2]),
                  big[:, OUT_OFF:OUT_OFF + 8 * S].rearrange(
                      "p (h s) -> p h s", s=S))
        wait("gpsimd", "agw", agv)
        ccv = bump("cc", 1)
        emit("gpsimd", lambda E, src=ag_in[half * 8:(half + 1) * 8], dst=ag_out_t: E.collective_compute(
            "AllGather", ALU.bypass, ins=[src], outs=[dst],
            replica_groups=[list(range(NCORE))]).then_inc(SEM["cc"], 1))

    def emit_agread_switch():
        g_done = cnt["g"]
        for arm in nc.switch_core_id(engines=[nc.sync], n=128):
            c = arm.logical % NCORE
            ag_t = ag_outA if c < 4 else ag_outB
            hbase = 2 * c if c < 4 else 2 * c - 8
            nc.sync.wait_ge(SEM["g"], g_done)
            nc.sync.wait_ge(SEM["cc"], 1 if c < 4 else 2)
            for r in range(NCORE):
                if r > 0:
                    nc.sync.wait_ge(SEM["agr"], cnt["agr"] + 16 * r)
                nc.sync.dma_start(
                    out=big[:, 0:16 * S].rearrange(
                        "p (hs rs) -> p hs rs", hs=2)[:, :, r * S:(r + 1) * S],
                    in_=ag_t[r, hbase:hbase + 2].transpose([1, 0, 2]),
                ).then_inc(SEM["agr"], 16)
        bump("agr", 16 * NCORE)

    WO_SLOTS = None

    def wo_load(o):
        slots = [(wbufQ[0][:], "wq0", "qbuf0"), (wbufQ[1][:], "wq1", "qbuf1"),
                 (wbufD[0][:].bitcast(BF16)[:, 0:KC * 128], "wd0", "dbuf0"),
                 (wbufD[1][:].bitcast(BF16)[:, 0:KC * 128], "wd1", "dbuf1")]
        buf, qsem, key = slots[o % 4]
        res_wait("gpsimd", key)
        qv = dma("gpsimd", qsem,
                 buf.rearrange("p (c o) -> p c o", o=128),
                 woT[:, o * 128:(o + 1) * 128].rearrange("(c p) o -> p c o", p=128))
        return (buf, qsem, key, qv)

    def wo_prefetch():
        nonlocal WO_SLOTS
        WO_SLOTS = {o: wo_load(o) for o in range(4)}

    def rows_wo_phase():
        LN_PEND = []
        RT_OFF = 16 * S
        wait("vector", "agr", cnt["agr"])
        for jb in range(16):
            for sel in range(2):
                om = big[:, sel * 8 * S:(sel + 1) * 8 * S]
                src = om.rearrange("p (u j) -> p u j", j=16)[:, :, jb]
                dst = big[:, RT_OFF + jb * S + sel * 512:
                          RT_OFF + jb * S + sel * 512 + 512]
                vop("tensor_copy", dst, src)
        rows_v = cnt["v"]
        wait("tensor", "v", rows_v)
        for o in range(OC):
            sl = o % 2
            if o in WO_SLOTS:
                buf, qsem, key, qv = WO_SLOTS[o]
            else:
                buf, qsem, key, qv = wo_load(o)
            res_wait("sync", f"xs{sl}")
            lv = dma("sync", f"xs{sl}", xs[sl][:], x1s[o])
            wait("tensor", qsem, qv)
            psW = PS[o % 2]
            res_wait("tensor", f"PS{o % 2}")
            for h0 in (H0, H1):
                for k in range(KC):
                    last = (h0 == H1) and (k == KC - 1)
                    mm(psW[:, h0:h0 + 512], buf[:, k * 128:(k + 1) * 128],
                       big[:, RT_OFF + k * S + h0:RT_OFF + k * S + h0 + 512],
                       start=(k == 0), stop=(k == KC - 1), inc=last)
            tmark = cnt["t"]
            res_set(key, "t", tmark)
            wait("scalar", "t", tmark)
            tt = scrF[2]
            res_wait("scalar", "wo_tt")
            sv = act(tt[:], psW[:], ACT.Identity, bias=biases[:, 32 + o:32 + o + 1])
            res_set(f"PS{o % 2}", "s", sv)
            wait("vector", "s", sv)
            wait("vector", f"xs{sl}", lv)
            res_wait("vector", f"stg{sl}")
            vv = vop("tensor_add", stg[sl][:], tt[:], xs[sl][:])
            res_set("wo_tt", "v", vv)
            res_set(f"xs{sl}", "v", vv)
            wait("sync", "v", vv)
            stv = dma("sync", f"st{sl}", x2s[o], stg[sl][:])
            res_set(f"stg{sl}", f"st{sl}", stv)
            sv2, vv2 = ln_chunk_ew(o, stg[sl][:], src_is_sbuf=True,
                                   src_marks=[("v", vv)])
            res_add(f"stg{sl}", "s", sv2)
            LN_PEND.append((o, sv2, vv2))
            if len(LN_PEND) > 1:
                ln_stats_mm(*LN_PEND.pop(0))
        while LN_PEND:
            ln_stats_mm(*LN_PEND.pop(0))

    v_ln1 = ln_phase("xfm", xfm)
    f1 = ffn_gateup(v_ln1)
    ffn_down(f1, "xfm", xfm, "x1s", x1s, ln_follow=True)
    v_ln2 = ln_tail()
    qkv_phase(v_ln2)
    feature_phase()
    a_out_phase(0, 8)
    wo_prefetch()
    exchange_half(0, ag_outA)
    a_out_phase(8, NH, vec_heads=(8, 9, 10))
    exchange_half(1, ag_outB)

    with nc.Block() as block:
        @block.tensor
        def _(E):
            for f in ops["tensor"]:
                f(E)

        @block.vector
        def _(E):
            for f in ops["vector"]:
                f(E)

        @block.scalar
        def _(E):
            for f in ops["scalar"]:
                f(E)

        @block.sync
        def _(E):
            for f in ops["sync"]:
                f(E)

        @block.gpsimd
        def _(E):
            for f in ops["gpsimd"]:
                f(E)

    emit_agread_switch()

    for k in ops:
        ops[k] = []
        water[k] = {}

    rows_wo_phase()
    v_ln3 = ln_tail()
    f2 = ffn_gateup(v_ln3)
    ffn_down(f2, "x2s", x2s, "ofm", ofm, ln_follow=False)
    wait("sync", "st0", cnt["st0"])
    wait("sync", "st1", cnt["st1"])

    with nc.Block() as block2:
        @block2.tensor
        def _(E):
            for f in ops["tensor"]:
                f(E)

        @block2.vector
        def _(E):
            for f in ops["vector"]:
                f(E)

        @block2.scalar
        def _(E):
            for f in ops["scalar"]:
                f(E)

        @block2.sync
        def _(E):
            for f in ops["sync"]:
                f(E)

        @block2.gpsimd
        def _(E):
            for f in ops["gpsimd"]:
                f(E)

    ctx.close()
    return nc


def _host_prep(inputs):
    x = np.asarray(inputs["L0_attn_hidden_states"])
    Wq = np.ascontiguousarray(np.asarray(inputs["L0_attn_Wq"]).T)
    Wk = np.ascontiguousarray(np.asarray(inputs["L0_attn_Wk"]).T)
    Wv = np.ascontiguousarray(np.asarray(inputs["L0_attn_Wv"]).T)
    Wo = np.ascontiguousarray(np.asarray(inputs["L0_attn_Wo"]).T)
    bq = np.ascontiguousarray(np.asarray(inputs["L0_attn_bq"]).reshape(NH, 128).T)
    bk = np.ascontiguousarray(np.asarray(inputs["L0_attn_bk"]).reshape(NKV, 128).T)
    bv = np.ascontiguousarray(np.asarray(inputs["L0_attn_bv"]).reshape(NKV, 128).T)
    bo = np.ascontiguousarray(np.asarray(inputs["L0_attn_bo"]).reshape(OC, 128).T)
    R = np.asarray(inputs["L0_attn_rand_matrix"])
    gw = np.ascontiguousarray(np.asarray(inputs["L0_ffn_gate_w"]))
    uw = np.ascontiguousarray(np.asarray(inputs["L0_ffn_up_w"]))
    dw = np.ascontiguousarray(np.asarray(inputs["L0_ffn_down_w"]))
    gs = np.ascontiguousarray(np.asarray(inputs["L0_ffn_gate_scale"]).reshape(IC, 128).T)
    us = np.ascontiguousarray(np.asarray(inputs["L0_ffn_up_scale"]).reshape(IC, 128).T)
    ds = np.ascontiguousarray(np.asarray(inputs["L0_ffn_down_scale"]).reshape(OC, 128).T)

    RT = np.ascontiguousarray(R.T)
    R2T = np.ascontiguousarray(np.roll(R, -64, axis=1).T)
    rtdup = np.ascontiguousarray(np.concatenate([RT, RT], axis=1))
    r2tdup = np.ascontiguousarray(np.concatenate([R2T, R2T], axis=1))

    cmean = np.full((128, 128), 1.0 / 2048.0, np.float32)
    cone = np.full((128, 1), 1.0, np.float32)
    caf = np.full((128, 128), 2.0 / 128.0, np.float32)
    crows = np.ones((1, 512), np.float32)
    cshift = np.concatenate([np.zeros(64), np.full(64, math.pi / 2)]).astype(np.float32).reshape(1, 128)
    ceps = np.full((1, 128), 1e-12, np.float32)

    in_maps = []
    for c in range(NCORE):
        xs = np.ascontiguousarray(x[0, c * S:(c + 1) * S, :].T)
        pos = np.arange(c * S, (c + 1) * S, dtype=np.float32)
        invf = (1.0 / (np.float32(THETA) ** (np.arange(0, HD, 2, dtype=np.float32) / np.float32(HD)))).astype(np.float32)
        freqs = pos[None, :] * invf[:, None]
        emb = np.concatenate([freqs, freqs], axis=0)
        cosz = np.cos(emb).astype(np.float32)
        sin_emb = np.sin(emb).astype(np.float32)
        dd = np.arange(HD)[:, None]
        sin2 = np.where(dd < 64, -sin_emb, sin_emb)
        sinp = np.ascontiguousarray(np.roll(sin2, -64, axis=0)).astype(np.float32)
        in_maps.append({
            "xfm": xs, "wqT": Wq, "wkT": Wk, "wvT": Wv, "woT": Wo,
            "gw": gw, "uw": uw, "dw": dw,
            "cosz": cosz, "sinp": sinp,
            "rtdup": rtdup, "r2tdup": r2tdup,
            "bqv": bq, "bkv": bk, "bvv": bv, "bov": bo,
            "gsv": gs, "usv": us, "dsv": ds,
            "cmean": cmean, "cone": cone, "caf": caf,
            "crows": crows, "cshift": cshift, "ceps": ceps,
        })
    return in_maps


def kernel(**inputs):
    in_maps = _host_prep(inputs)
    nc = build_nc()
    res = run_bass_kernel_spmd(nc, in_maps, core_ids=list(range(NCORE)))
    out = np.empty((1, SF, D), np.float32)
    for c in range(NCORE):
        ofm = np.asarray(res.results[c]["ofm"])
        out[0, c * S:(c + 1) * S, :] = ofm.reshape(D, S).T
    return out


# version 17 (final)
# speedup vs baseline: 1.0291x; 1.0291x over previous
import math
import numpy as np
import ml_dtypes

import concourse.bass as bass
import concourse.mybir as mybir
from concourse.bass_utils import run_bass_kernel_spmd

F32 = mybir.dt.float32
BF16 = mybir.dt.bfloat16
FP8 = mybir.dt.float8e4
I32 = mybir.dt.int32
ACT = mybir.ActivationFunctionType
ALU = mybir.AluOpType

NCORE = 8
D = 2048
S = 1024
SF = 8192
I_FF = 6144
KC = D // 128
IC = I_FF // 128
OC = D // 128
NH, NKV, HD = 16, 8, 128
NHEADS = NH + NKV
THETA = 10000.0
MAGIC = float(1.5 * 2 ** 23)
INV2PI = float(1.0 / (2.0 * math.pi))
TWOPI = float(2.0 * math.pi)

H0, H1 = 0, 512


def build_nc(debug_sync=False):
    nc = bass.Bass("TRN2", target_bir_lowering=False, num_devices=NCORE,
                   detect_race_conditions=debug_sync)

    xfm = nc.dram_tensor("xfm", [D, S], F32, kind="ExternalInput").ap()
    wqT = nc.dram_tensor("wqT", [D, D], F32, kind="ExternalInput").ap()
    wkT = nc.dram_tensor("wkT", [D, NKV * HD], F32, kind="ExternalInput").ap()
    wvT = nc.dram_tensor("wvT", [D, NKV * HD], F32, kind="ExternalInput").ap()
    woT = nc.dram_tensor("woT", [D, D], F32, kind="ExternalInput").ap()
    gw = nc.dram_tensor("gw", [D, I_FF], F32, kind="ExternalInput").ap()
    uw = nc.dram_tensor("uw", [D, I_FF], F32, kind="ExternalInput").ap()
    dw = nc.dram_tensor("dw", [I_FF, D], F32, kind="ExternalInput").ap()
    cosz = nc.dram_tensor("cosz", [128, S], F32, kind="ExternalInput").ap()
    sinp = nc.dram_tensor("sinp", [128, S], F32, kind="ExternalInput").ap()
    rtdup = nc.dram_tensor("rtdup", [128, 128], F32, kind="ExternalInput").ap()
    r2tdup = nc.dram_tensor("r2tdup", [128, 128], F32, kind="ExternalInput").ap()
    bqv = nc.dram_tensor("bqv", [128, NH], F32, kind="ExternalInput").ap()
    bkv = nc.dram_tensor("bkv", [128, NKV], F32, kind="ExternalInput").ap()
    bvv = nc.dram_tensor("bvv", [128, NKV], F32, kind="ExternalInput").ap()
    bov = nc.dram_tensor("bov", [128, OC], F32, kind="ExternalInput").ap()
    gsv = nc.dram_tensor("gsv", [128, IC], F32, kind="ExternalInput").ap()
    usv = nc.dram_tensor("usv", [128, IC], F32, kind="ExternalInput").ap()
    dsv = nc.dram_tensor("dsv", [128, OC], F32, kind="ExternalInput").ap()
    cmean = nc.dram_tensor("cmean", [128, 128], F32, kind="ExternalInput").ap()
    cone = nc.dram_tensor("cone", [128, 1], F32, kind="ExternalInput").ap()
    caf = nc.dram_tensor("caf", [128, 128], F32, kind="ExternalInput").ap()
    crows = nc.dram_tensor("crows", [1, 512], F32, kind="ExternalInput").ap()
    cshift = nc.dram_tensor("cshift", [1, 128], F32, kind="ExternalInput").ap()
    ceps = nc.dram_tensor("ceps", [1, 128], F32, kind="ExternalInput").ap()

    ofm = nc.dram_tensor("ofm", [OC, 128, S], F32, kind="ExternalOutput").ap()

    x1s = nc.dram_tensor("x1s", [KC, 128, S], F32, kind="Internal").ap()
    x2s = nc.dram_tensor("x2s", [KC, 128, S], F32, kind="Internal").ap()
    ag_in = nc.dram_tensor("ag_in", [NH, 128, S], BF16, kind="Internal").ap()
    ag_outA = nc.dram_tensor("ag_outA", [NCORE, 8, 128, S], BF16,
                             kind="Internal").ap()
    ag_outB = nc.dram_tensor("ag_outB", [NCORE, 8, 128, S], BF16,
                             kind="Internal").ap()

    from contextlib import ExitStack
    ctx = ExitStack()
    e = ctx.enter_context

    xb = e(nc.sbuf_tensor("xb", [128, KC * S], BF16))
    big = e(nc.sbuf_tensor("big", [128, 48 * S], BF16))
    cosT = e(nc.sbuf_tensor("cosT", [128, S], BF16))
    sinP = e(nc.sbuf_tensor("sinP", [128, S], BF16))
    xs = [e(nc.sbuf_tensor(f"xs{i}", [128, S], F32)) for i in range(2)]
    stg = [e(nc.sbuf_tensor(f"stg{i}", [128, S], F32)) for i in range(2)]
    wbufA = [e(nc.sbuf_tensor(f"wbA{i}", [128, KC * 128], FP8)) for i in range(2)]
    wbufB = [e(nc.sbuf_tensor(f"wbB{i}", [128, KC * 128], FP8)) for i in range(2)]
    wbufD = [e(nc.sbuf_tensor(f"wbD{i}", [128, IC * 128], FP8)) for i in range(2)]
    wbufQ = [e(nc.sbuf_tensor(f"wbQ{i}", [128, KC * 128], BF16)) for i in range(2)]
    scrF = [e(nc.sbuf_tensor(f"scrF{i}", [128, S], F32)) for i in range(5)]
    scrB = [e(nc.sbuf_tensor(f"scrB{i}", [128, S], BF16)) for i in range(4)]
    biases = e(nc.sbuf_tensor("biases", [128, 64], F32))
    dsh = e(nc.sbuf_tensor("dsh", [128, OC], F32))
    consts = e(nc.sbuf_tensor("consts", [128, 258], BF16))
    rt_bf = e(nc.sbuf_tensor("rt_bf", [128, 2 * 128], BF16))
    rows_bf = e(nc.sbuf_tensor("rows_bf", [1, 512 + 2 * 128], BF16))
    onesf = e(nc.sbuf_tensor("onesf", [1, 128], F32))

    psA = e(nc.psum_tensor("psA", [128, S], F32))
    psB = e(nc.psum_tensor("psB", [128, S], F32))
    psC = e(nc.psum_tensor("psC", [128, S], F32))
    psD = e(nc.psum_tensor("psD", [128, S], F32))
    PS = [psA, psB, psC, psD]

    gsus = e(nc.sbuf_tensor("gsus", [128, 2 * IC], F32))

    sem_names = ["t", "v", "s", "cc", "pre", "preg",
                 "wg0", "wg1", "wu0", "wu1", "wd0", "wd1", "wq0", "wq1",
                 "xs0", "xs1", "st0", "st1", "agw", "agr", "nsg0", "nsg1", "nsc0", "nsc1", "g"]
    SEM = {n: e(nc.semaphore(n)) for n in sem_names}
    cnt = {n: 0 for n in sem_names}

    ops = {k: [] for k in ("tensor", "vector", "scalar", "sync", "gpsimd")}
    water = {k: {} for k in ops}

    def emit(eng, fn):
        ops[eng].append(fn)

    def wait(eng, sem, val):
        if val <= 0:
            return
        if water[eng].get(sem, -1) == val:
            return
        water[eng][sem] = val
        emit(eng, lambda E, s=sem, v=val: E.wait_ge(SEM[s], v))

    def bump(sem, by):
        cnt[sem] += by
        return cnt[sem]

    marks = {}

    def res_wait(eng, res):
        for sem, val in marks.get(res, ()):
            wait(eng, sem, val)

    def res_set(res, sem, val):
        marks[res] = [(sem, val)]

    def res_add(res, sem, val):
        marks.setdefault(res, []).append((sem, val))

    def dma(eng, qsem, out_ap, in_ap):
        v = bump(qsem, 16)
        emit(eng, lambda E, o=out_ap, i=in_ap, s=qsem, vv=v:
             E.dma_start(out=o, in_=i).then_inc(SEM[s], 16))
        return v

    def act(out_ap, in_ap, func, bias=0.0, scale=1.0, inc=True):
        v = bump("s", 1) if inc else None
        def f(E, o=out_ap, i=in_ap, fn=func, b=bias, sc=scale, vv=v):
            ins = E.activation(out=o, in_=i, func=fn, bias=b, scale=sc)
            if vv is not None:
                ins.then_inc(SEM["s"], 1)
        emit("scalar", f)
        if debug_sync and v is not None:
            emit("scalar", lambda E, vv=v: E.wait_ge(SEM["s"], vv))
        return cnt["s"]

    def vop(name, *args, inc=True, **kw):
        v = bump("v", 1) if inc else None
        def f(E, n=name, a=args, k=kw, vv=v):
            ins = getattr(E, n)(*a, **k)
            if vv is not None:
                ins.then_inc(SEM["v"], 1)
        emit("vector", f)
        if debug_sync and v is not None:
            emit("vector", lambda E, vv=v: E.wait_ge(SEM["v"], vv))
        return cnt["v"]

    def gop(name, *args, inc=True, **kw):
        v = bump("g", 1) if inc else None
        def f(E, n=name, a=args, k=kw, vv=v):
            ins = getattr(E, n)(*a, **k)
            if vv is not None:
                ins.then_inc(SEM["g"], 1)
        emit("gpsimd", f)
        if debug_sync and v is not None:
            emit("gpsimd", lambda E, vv=v: E.wait_ge(SEM["g"], vv))
        return cnt["g"]

    def mm(out_ap, lhsT, rhs, start, stop, inc=False):
        v = bump("t", 1) if inc else None
        def f(E, o=out_ap, l=lhsT, r=rhs, st=start, sp=stop, vv=v):
            ins = E.matmul(o, l, r, start=st, stop=sp)
            if vv is not None:
                ins.then_inc(SEM["t"], 1)
        emit("tensor", f)
        return cnt["t"]

    pre_loads = [
        (biases[:, 0:NH], bqv), (biases[:, 16:16 + NKV], bkv),
        (biases[:, 24:24 + NKV], bvv), (biases[:, 32:32 + OC], bov),
        (biases[:, 48:48 + OC], dsv),
        (gsus[:, 0:IC], gsv), (gsus[:, IC:2 * IC], usv),
        (onesf[:, 0:128], crows[0:1, 0:128]),
    ]
    for o, i in pre_loads:
        dma("sync", "pre", o, i)
    cast_loads = [
        (cosT[:], cosz), (sinP[:], sinp),
        (consts[:, 0:128], cmean), (consts[:, 128:256], caf),
        (consts[:, 256:257], cone),
        (rt_bf[:, 0:128], rtdup), (rt_bf[:, 128:256], r2tdup),
        (rows_bf[:, 0:512], crows), (rows_bf[:, 512:512 + 128], cshift),
        (rows_bf[:, 512 + 128:512 + 256], ceps),
    ]
    for o, i in cast_loads:
        dma("gpsimd", "preg", o, i)
    npre = cnt["pre"]
    npreg = cnt["preg"]
    wait("scalar", "pre", npre)
    wait("scalar", "preg", npreg)
    emit("scalar", lambda E: E.preload_act_table(ACT.Silu))
    act(dsh[:], biases[:, 48:48 + OC], ACT.Copy, scale=0.5)
    wait("tensor", "pre", npre)
    wait("tensor", "preg", npreg)
    wait("vector", "pre", npre)
    wait("vector", "preg", npreg)

    C_MEAN = consts[:, 0:128]
    C_AF = consts[:, 128:256]
    C_ONE = consts[:, 256:258]
    RT_BF = rt_bf[:, 0:128]
    R2T_BF = rt_bf[:, 128:256]
    ROW1 = rows_bf[:, 0:512]
    SHIFTC = rows_bf[0:1, 512:512 + 128]
    EPSC = rows_bf[0:1, 512 + 128:512 + 256]
    ONESCOL_F = onesf[0:1, :]

    def newton_rsqrt(out_t, val_ap, u_t, iters=3):
        oi = out_t[:].bitcast(I32)
        vop("tensor_scalar", oi, val_ap.bitcast(I32), 1, None, ALU.arith_shift_right)
        vop("tensor_scalar", oi, oi, -1, 0x5F3759DF, ALU.mult, ALU.add)
        for _ in range(iters):
            vop("tensor_mul", u_t[:], out_t[:], out_t[:])
            vop("scalar_tensor_tensor", u_t[:], val_ap, -0.5, u_t[:], ALU.mult, ALU.mult)
            vop("scalar_tensor_tensor", out_t[:], u_t[:], 1.5, out_t[:], ALU.add, ALU.mult)

    def ln_chunk_ew(k, src, src_is_sbuf=False, src_marks=()):
        if src_is_sbuf:
            for sem, val in src_marks:
                wait("scalar", sem, val)
            xsrc = src
        else:
            sl = k % 2
            res_wait("sync", f"xs{sl}")
            lv = dma("sync", f"xs{sl}", xs[sl][:], src)
            wait("scalar", f"xs{sl}", lv)
            xsrc = xs[sl][:]
        xbk = xb[:, k * S:(k + 1) * S]
        sv = act(xbk, xsrc, ACT.Copy)
        if not src_is_sbuf:
            res_set(f"xs{k % 2}", "s", sv)
        sq = scrB[2 + (k % 2)]
        res_wait("vector", f"SB{2 + (k % 2)}")
        wait("vector", "s", sv)
        vv = vop("tensor_mul", sq[:], xbk, xbk)
        return sv, vv

    def ln_stats_mm(k, sv, vv):
        xbk = xb[:, k * S:(k + 1) * S]
        sq = scrB[2 + (k % 2)]
        wait("tensor", "s", sv)
        wait("tensor", "v", vv)
        if k == 0:
            res_wait("tensor", "PS2")
            res_wait("tensor", "PS3")
        for h0 in (H0, H1):
            mm(psC[:, h0:h0 + 512], C_MEAN, xbk[:, h0:h0 + 512],
               start=(k == 0), stop=(k == KC - 1))
        for h0 in (H0, H1):
            last = h0 == H1
            mm(psD[:, h0:h0 + 512], C_MEAN, sq[:, h0:h0 + 512],
               start=(k == 0), stop=(k == KC - 1), inc=last)
        res_set(f"SB{2 + (k % 2)}", "t", cnt["t"])

    def ln_chunk(k, src, src_is_sbuf=False, src_marks=()):
        sv, vv = ln_chunk_ew(k, src, src_is_sbuf, src_marks)
        ln_stats_mm(k, sv, vv)
        return sv

    def ln_tail():
        wait("vector", "t", cnt["t"])
        mean_t, rstd_t = scrF[3], scrF[4]
        vop("tensor_copy", mean_t[:], psC[:])
        var_t, u_t = scrF[0], scrF[1]
        vop("scalar_tensor_tensor", var_t[:], mean_t[:], -1.0, mean_t[:],
            ALU.mult, ALU.mult)
        vop("tensor_add", var_t[:], var_t[:], psD[:])
        vv = vop("tensor_scalar", var_t[:], var_t[:], 1e-5, None, ALU.add)
        res_set("PS2", "v", vv)
        res_set("PS3", "v", vv)
        newton_rsqrt(rstd_t, var_t[:], u_t)
        for k in range(KC):
            xbk = xb[:, k * S:(k + 1) * S]
            t0 = scrF[2]
            vop("scalar_tensor_tensor", t0[:], xbk, 1.0, mean_t[:],
                ALU.mult, ALU.subtract)
            vop("tensor_mul", xbk, t0[:], rstd_t[:])
        return bump("v", 0)

    def ln_phase(src_kind, src_ap):
        for k in range(KC):
            if src_kind == "xfm":
                chunk = src_ap[k * 128:(k + 1) * 128, :]
            else:
                chunk = src_ap[k]
            ln_chunk(k, chunk)
        return ln_tail()

    def ffn_gateup(xbc_ready_v):
        wait("tensor", "v", xbc_ready_v)
        for i in range(IC):
            sl = i % 2
            res_wait("gpsimd", f"gbuf{sl}")
            gv = dma("gpsimd", f"wg{sl}",
                     wbufA[sl][:].rearrange("p (c i) -> p c i", i=128),
                     gw[:, i * 128:(i + 1) * 128].rearrange("(c p) i -> p c i", p=128))
            res_wait("gpsimd", f"ubuf{sl}")
            uv = dma("gpsimd", f"wu{sl}",
                     wbufB[sl][:].rearrange("p (c i) -> p c i", i=128),
                     uw[:, i * 128:(i + 1) * 128].rearrange("(c p) i -> p c i", p=128))
            wait("tensor", f"wg{sl}", gv)
            wait("tensor", f"wu{sl}", uv)
            psG = PS[i % 2]
            psU = PS[2 + (i % 2)]
            res_wait("tensor", f"PS{i % 2}")
            res_wait("tensor", f"PS{2 + (i % 2)}")
            for h0 in (H0, H1):
                for k in range(KC):
                    mm(psG[:, h0:h0 + 512], wbufA[sl][:, k * 128:(k + 1) * 128],
                       xb[:, k * S + h0:k * S + h0 + 512],
                       start=(k == 0), stop=(k == KC - 1))
            for h0 in (H0, H1):
                for k in range(KC):
                    last = (h0 == H1) and (k == KC - 1)
                    mm(psU[:, h0:h0 + 512], wbufB[sl][:, k * 128:(k + 1) * 128],
                       xb[:, k * S + h0:k * S + h0 + 512],
                       start=(k == 0), stop=(k == KC - 1), inc=last)
            tmark = cnt["t"]
            res_set(f"gbuf{sl}", "t", tmark)
            res_set(f"ubuf{sl}", "t", tmark)
            wait("scalar", "t", tmark)
            gact, ups = scrB[0], scrB[1]
            res_wait("scalar", "gact")
            res_wait("scalar", "ups")
            act(gact[:], psG[:], ACT.Silu, scale=gsus[:, i:i + 1])
            sv = act(ups[:], psU[:], ACT.Copy, scale=gsus[:, IC + i:IC + i + 1])
            res_set(f"PS{i % 2}", "s", sv)
            res_set(f"PS{2 + (i % 2)}", "s", sv)
            wait("vector", "s", sv)
            fv = vop("tensor_mul", big[:, i * S:(i + 1) * S], gact[:], ups[:])
            res_set("gact", "v", fv)
            res_set("ups", "v", fv)
        return cnt["v"]

    def ffn_down(f_ready_v, src_kind, src_ap, dst_kind, dst_ap, ln_follow):
        LN_PEND = []
        wait("tensor", "v", f_ready_v)
        for o in range(OC):
            sl = o % 2
            res_wait("gpsimd", f"dbuf{sl}")
            dv = dma("gpsimd", f"wd{sl}",
                     wbufD[sl][:].rearrange("p (c o) -> p c o", o=128),
                     dw[:, o * 128:(o + 1) * 128].rearrange("(c p) o -> p c o", p=128))
            res_wait("sync", f"xs{sl}")
            if src_kind == "xfm":
                chunk = src_ap[o * 128:(o + 1) * 128, :]
            else:
                chunk = src_ap[o]
            lv = dma("sync", f"xs{sl}", xs[sl][:], chunk)
            wait("tensor", f"wd{sl}", dv)
            psO = PS[o % 2]
            res_wait("tensor", f"PS{o % 2}")
            for h0 in (H0, H1):
                for i in range(IC):
                    last = (h0 == H1) and (i == IC - 1)
                    mm(psO[:, h0:h0 + 512], wbufD[sl][:, i * 128:(i + 1) * 128],
                       big[:, i * S + h0:i * S + h0 + 512],
                       start=(i == 0), stop=(i == IC - 1), inc=last)
            tmark = cnt["t"]
            res_set(f"dbuf{sl}", "t", tmark)
            wait("vector", "t", tmark)
            wait("vector", f"xs{sl}", lv)
            res_wait("vector", f"stg{sl}")
            vv = vop("scalar_tensor_tensor", stg[sl][:], psO[:],
                     dsh[:, o:o + 1], xs[sl][:], ALU.mult, ALU.add)
            res_set(f"PS{o % 2}", "v", vv)
            res_set(f"xs{sl}", "v", vv)
            wait("sync", "v", vv)
            stv = dma("sync", f"st{sl}", dst_ap[o], stg[sl][:])
            res_set(f"stg{sl}", f"st{sl}", stv)
            if ln_follow:
                sv2, vv2 = ln_chunk_ew(o, stg[sl][:], src_is_sbuf=True,
                                       src_marks=[("v", vv)])
                res_add(f"stg{sl}", "s", sv2)
                LN_PEND.append((o, sv2, vv2))
                if len(LN_PEND) > 1:
                    ln_stats_mm(*LN_PEND.pop(0))
        if ln_follow:
            while LN_PEND:
                ln_stats_mm(*LN_PEND.pop(0))
    def qkv_phase(xbc_ready_v):
        wait("tensor", "v", xbc_ready_v)
        plan = []
        for h in range(NH):
            plan.append((wqT, h, biases[:, h:h + 1]))
        for j in range(NKV):
            plan.append((wkT, j, biases[:, 16 + j:16 + j + 1]))
        for j in range(NKV):
            plan.append((wvT, j, biases[:, 24 + j:24 + j + 1]))
        for idx, (wt, col, bias_col) in enumerate(plan):
            sl = idx % 2
            res_wait("gpsimd", f"qbuf{sl}")
            qv = dma("gpsimd", f"wq{sl}",
                     wbufQ[sl][:].rearrange("p (c o) -> p c o", o=128),
                     wt[:, col * 128:(col + 1) * 128].rearrange("(c p) o -> p c o", p=128))
            wait("tensor", f"wq{sl}", qv)
            psQ = PS[idx % 2]
            res_wait("tensor", f"PS{idx % 2}")
            for h0 in (H0, H1):
                for k in range(KC):
                    last = (h0 == H1) and (k == KC - 1)
                    mm(psQ[:, h0:h0 + 512], wbufQ[sl][:, k * 128:(k + 1) * 128],
                       xb[:, k * S + h0:k * S + h0 + 512],
                       start=(k == 0), stop=(k == KC - 1), inc=last)
            tmark = cnt["t"]
            res_set(f"qbuf{sl}", "t", tmark)
            wait("scalar", "t", tmark)
            sv = act(big[:, idx * S:(idx + 1) * S], psQ[:], ACT.Identity,
                     bias=bias_col)
            res_set(f"PS{idx % 2}", "s", sv)

    def feature_phase():
        nsq24 = scrF[3]
        rowb = [scrF[4][0:1, :], scrF[0][0:1, :]]
        wait("scalar", "s", cnt["s"])
        l1 = {}

        def l1_stageA(h):
            qh = big[:, h * S:(h + 1) * S]
            sq = scrB[2 + (h % 2)]
            res_wait("scalar", f"SB{2 + (h % 2)}")
            sv = act(sq[:], qh, ACT.Square)
            psN = PS[h % 4]
            res_wait("tensor", f"PS{h % 4}")
            wait("tensor", "s", sv)
            for h0 in (H0, H1):
                mm(psN[0:1, h0:h0 + 512], C_ONE[:, 0:1], sq[:, h0:h0 + 512],
                   start=True, stop=False)
                last = h0 == H1
                mm(psN[0:1, h0:h0 + 512], EPSC[0:1, 0:1], ROW1[0:1, 0:512],
                   start=False, stop=True, inc=last)
            l1[h] = cnt["t"]
            res_set(f"SB{2 + (h % 2)}", "t", l1[h])

        def l1_stageB(h):
            psN = PS[h % 4]
            wait("scalar", "t", l1[h])
            res_wait("scalar", f"rowb{h % 2}")
            sv = act(rowb[h % 2], psN[0:1, :], ACT.Copy)
            res_set(f"PS{h % 4}", "s", sv)
            wait("sync", "s", sv)
            gv = dma("sync", f"nsg{h % 2}", nsq24[h:h + 1, :], rowb[h % 2])
            res_set(f"rowb{h % 2}", f"nsg{h % 2}", gv)

        for h in range(NHEADS):
            l1_stageA(h)
            if h >= 1:
                l1_stageB(h - 1)
        l1_stageB(NHEADS - 1)
        wait("vector", "nsg0", cnt["nsg0"])
        wait("vector", "nsg1", cnt["nsg1"])
        y24 = scrF[2][0:24, :]
        u24 = scrF[0][0:24, :]
        val = nsq24[0:24, :]
        yi = y24.bitcast(I32)
        vop("tensor_scalar", yi, val.bitcast(I32), 1, None, ALU.arith_shift_right)
        vop("tensor_scalar", yi, yi, -1, 0x5F3759DF, ALU.mult, ALU.add)
        for it in range(3):
            vop("tensor_mul", u24, y24, y24)
            vop("scalar_tensor_tensor", u24, val, -0.5, u24, ALU.mult, ALU.mult)
            dst = val if it == 2 else y24
            nv = vop("scalar_tensor_tensor", dst, u24, 1.5, y24, ALU.add, ALU.mult)
        wait("sync", "st0", cnt["st0"])
        wait("sync", "st1", cnt["st1"])
        ld_ones32 = dma("sync", "pre", stg[0][32:33, 0:128], crows[0:1, 0:128])
        rowb2s = [(scrF[4][0:1, :], ONESCOL_F, "nsc0", "rowb2A"),
                  (stg[1][32:33, :], stg[0][32:33, 0:128], "nsc1", "rowb2B")]
        rk_bufs = [(scrF[2], "rkb0"), (xs[0], "xs0")]
        rr_bufs = [(scrF[1], "rrb0"), (xs[1], "xs1")]
        fst = {}

        def fm_stage1(h):
            rowb2, onesc, nscs, rkey = rowb2s[h % 2]
            res_wait("sync", rkey)
            wait("sync", "v", nv)
            wait("sync", "pre", ld_ones32)
            cv = dma("sync", nscs, rowb2, nsq24[h:h + 1, :])
            psN = PS[(h % 2) * 2]
            res_wait("tensor", f"PS{(h % 2) * 2}")
            wait("tensor", "pre", ld_ones32)
            wait("tensor", nscs, cv)
            for h0 in (H0, H1):
                last = h0 == H1
                mm(psN[:, h0:h0 + 512], onesc, rowb2[0:1, h0:h0 + 512],
                   start=True, stop=True, inc=last)
            fst[h] = {"bv_t": cnt["t"]}
            res_set(rkey, "t", fst[h]["bv_t"])

        def fm_stage2(h):
            qh = big[:, h * S:(h + 1) * S]
            psN = PS[(h % 2) * 2]
            wait("vector", "t", fst[h]["bv_t"])
            tq = scrF[0]
            tqv = vop("tensor_mul", tq[:], qh, psN[:])
            res_set(f"PS{(h % 2) * 2}", "v", tqv)
            if h >= 1:
                wait("vector", "t", fst[h - 1]["tpv"])
            t1, t2 = scrB[0], scrB[1]
            vop("tensor_mul", t1[:], tq[:], cosT[:])
            fst[h]["nv2"] = vop("tensor_mul", t2[:], tq[:], sinP[:])

        def fm_stage3(h):
            psP = PS[(h % 2) * 2 + 1]
            t1, t2 = scrB[0], scrB[1]
            res_wait("tensor", f"PS{(h % 2) * 2 + 1}")
            wait("tensor", "v", fst[h]["nv2"])
            for h0 in (H0, H1):
                mm(psP[:, h0:h0 + 512], RT_BF, t1[:, h0:h0 + 512],
                   start=True, stop=False)
                mm(psP[:, h0:h0 + 512], R2T_BF, t2[:, h0:h0 + 512],
                   start=False, stop=False)
                last = h0 == H1
                mm(psP[:, h0:h0 + 512], SHIFTC, ROW1[0:1, 0:512],
                   start=False, stop=True, inc=last)
            fst[h]["tpv"] = cnt["t"]
            rkb, rk_res = rk_bufs[h % 2]
            res_wait("scalar", rk_res)
            wait("scalar", "t", fst[h]["tpv"])
            act(rkb[:], psP[:], ACT.Copy, scale=INV2PI, bias=MAGIC)
            fst[h]["rk2v"] = act(rkb[:], rkb[:], ACT.Copy, bias=-MAGIC)

        def fm_stage4(h):
            psP = PS[(h % 2) * 2 + 1]
            rkb, rk_res = rk_bufs[h % 2]
            rrb, rr_res = rr_bufs[h % 2]
            res_wait("vector", rr_res)
            wait("vector", "s", fst[h]["rk2v"])
            rv = vop("scalar_tensor_tensor", rrb[:], rkb[:], -TWOPI, psP[:],
                     ALU.mult, ALU.add)
            res_set(f"PS{(h % 2) * 2 + 1}", "v", rv)
            res_set(rk_res, "v", rv)
            wait("scalar", "v", rv)
            sv = act(big[:, h * S:(h + 1) * S], rrb[:], ACT.Sin)
            res_set(rr_res, "s", sv)
            del fst[h]

        for h in range(NHEADS):
            fm_stage1(h)
            if h >= 1:
                fm_stage2(h - 1)
                fm_stage3(h - 1)
            if h >= 2:
                fm_stage4(h - 2)
        fm_stage2(NHEADS - 1)
        fm_stage3(NHEADS - 1)
        fm_stage4(NHEADS - 2)
        fm_stage4(NHEADS - 1)

    def a_out_phase(h_lo, h_hi):
        OUT_OFF = 32 * S
        wait("gpsimd", "s", cnt["s"])
        for h in range(h_lo, h_hi):
            qp = big[:, h * S:(h + 1) * S]
            outacc = [scrF[0], scrF[1]][h % 2]
            for j in range(NKV):
                kp = big[:, (NH + j) * S:(NH + j + 1) * S]
                vj = big[:, (NH + NKV + j) * S:(NH + NKV + j + 1) * S]
                prod = scrB[2 + (j % 2)]
                res_wait("gpsimd", f"SB{2 + (j % 2)}")
                pv = gop("tensor_mul", prod[:], qp, kp)
                psX = PS[j % 2]
                res_wait("tensor", f"PS{j % 2}")
                wait("tensor", "g", pv)
                for h0 in (H0, H1):
                    last = h0 == H1
                    mm(psX[:, h0:h0 + 512], C_AF, prod[:, h0:h0 + 512],
                       start=True, stop=True, inc=last)
                tv = cnt["t"]
                res_set(f"SB{2 + (j % 2)}", "t", tv)
                wait("vector", "t", tv)
                if j == 0:
                    res_wait("vector", f"outacc{h % 2}")
                    av = vop("tensor_mul", outacc[:], psX[:], vj)
                else:
                    tmp = scrB[0]
                    vop("tensor_mul", tmp[:], psX[:], vj)
                    av = vop("tensor_add", outacc[:], outacc[:], tmp[:])
                res_set(f"PS{j % 2}", "v", av)
            wait("scalar", "v", cnt["v"])
            sv = act(big[:, OUT_OFF + h * S:OUT_OFF + (h + 1) * S], outacc[:],
                     ACT.Copy)
            res_set(f"outacc{h % 2}", "s", sv)

    def exchange_half(half, ag_out_t):
        OUT_OFF = 32 * S + half * 8 * S
        wait("sync", "s", cnt["s"])
        wait("sync", "agw", cnt["agw"])
        agv = dma("sync", "agw", ag_in[half * 8:(half + 1) * 8].transpose([1, 0, 2]),
                  big[:, OUT_OFF:OUT_OFF + 8 * S].rearrange(
                      "p (h s) -> p h s", s=S))
        wait("gpsimd", "agw", agv)
        ccv = bump("cc", 1)
        emit("gpsimd", lambda E, src=ag_in[half * 8:(half + 1) * 8], dst=ag_out_t: E.collective_compute(
            "AllGather", ALU.bypass, ins=[src], outs=[dst],
            replica_groups=[list(range(NCORE))]).then_inc(SEM["cc"], 1))

    def emit_agread_switch():
        g_done = cnt["g"]
        for arm in nc.switch_core_id(engines=[nc.sync], n=128):
            c = arm.logical % NCORE
            ag_t = ag_outA if c < 4 else ag_outB
            hbase = 2 * c if c < 4 else 2 * c - 8
            nc.sync.wait_ge(SEM["g"], g_done)
            nc.sync.wait_ge(SEM["cc"], 1 if c < 4 else 2)
            for r in range(NCORE):
                if r > 0:
                    nc.sync.wait_ge(SEM["agr"], cnt["agr"] + 16 * r)
                nc.sync.dma_start(
                    out=big[:, 0:16 * S].rearrange(
                        "p (hs rs) -> p hs rs", hs=2)[:, :, r * S:(r + 1) * S],
                    in_=ag_t[r, hbase:hbase + 2].transpose([1, 0, 2]),
                ).then_inc(SEM["agr"], 16)
        bump("agr", 16 * NCORE)

    WO_SLOTS = None

    def wo_load(o):
        slots = [(wbufQ[0][:], "wq0", "qbuf0"), (wbufQ[1][:], "wq1", "qbuf1"),
                 (wbufD[0][:].bitcast(BF16)[:, 0:KC * 128], "wd0", "dbuf0"),
                 (wbufD[1][:].bitcast(BF16)[:, 0:KC * 128], "wd1", "dbuf1")]
        buf, qsem, key = slots[o % 4]
        res_wait("gpsimd", key)
        qv = dma("gpsimd", qsem,
                 buf.rearrange("p (c o) -> p c o", o=128),
                 woT[:, o * 128:(o + 1) * 128].rearrange("(c p) o -> p c o", p=128))
        return (buf, qsem, key, qv)

    def wo_prefetch():
        nonlocal WO_SLOTS
        WO_SLOTS = {o: wo_load(o) for o in range(4)}

    def rows_wo_phase():
        LN_PEND = []
        RT_OFF = 16 * S
        wait("vector", "agr", cnt["agr"])
        for jb in range(16):
            for sel in range(2):
                om = big[:, sel * 8 * S:(sel + 1) * 8 * S]
                src = om.rearrange("p (u j) -> p u j", j=16)[:, :, jb]
                dst = big[:, RT_OFF + jb * S + sel * 512:
                          RT_OFF + jb * S + sel * 512 + 512]
                vop("tensor_copy", dst, src)
        rows_v = cnt["v"]
        wait("tensor", "v", rows_v)
        for o in range(OC):
            sl = o % 2
            if o in WO_SLOTS:
                buf, qsem, key, qv = WO_SLOTS[o]
            else:
                buf, qsem, key, qv = wo_load(o)
            res_wait("sync", f"xs{sl}")
            lv = dma("sync", f"xs{sl}", xs[sl][:], x1s[o])
            wait("tensor", qsem, qv)
            psW = PS[o % 2]
            res_wait("tensor", f"PS{o % 2}")
            for h0 in (H0, H1):
                for k in range(KC):
                    last = (h0 == H1) and (k == KC - 1)
                    mm(psW[:, h0:h0 + 512], buf[:, k * 128:(k + 1) * 128],
                       big[:, RT_OFF + k * S + h0:RT_OFF + k * S + h0 + 512],
                       start=(k == 0), stop=(k == KC - 1), inc=last)
            tmark = cnt["t"]
            res_set(key, "t", tmark)
            wait("scalar", "t", tmark)
            tt = scrF[2]
            res_wait("scalar", "wo_tt")
            sv = act(tt[:], psW[:], ACT.Identity, bias=biases[:, 32 + o:32 + o + 1])
            res_set(f"PS{o % 2}", "s", sv)
            wait("vector", "s", sv)
            wait("vector", f"xs{sl}", lv)
            res_wait("vector", f"stg{sl}")
            vv = vop("tensor_add", stg[sl][:], tt[:], xs[sl][:])
            res_set("wo_tt", "v", vv)
            res_set(f"xs{sl}", "v", vv)
            wait("sync", "v", vv)
            stv = dma("sync", f"st{sl}", x2s[o], stg[sl][:])
            res_set(f"stg{sl}", f"st{sl}", stv)
            sv2, vv2 = ln_chunk_ew(o, stg[sl][:], src_is_sbuf=True,
                                   src_marks=[("v", vv)])
            res_add(f"stg{sl}", "s", sv2)
            LN_PEND.append((o, sv2, vv2))
            if len(LN_PEND) > 1:
                ln_stats_mm(*LN_PEND.pop(0))
        while LN_PEND:
            ln_stats_mm(*LN_PEND.pop(0))

    v_ln1 = ln_phase("xfm", xfm)
    f1 = ffn_gateup(v_ln1)
    ffn_down(f1, "xfm", xfm, "x1s", x1s, ln_follow=True)
    v_ln2 = ln_tail()
    qkv_phase(v_ln2)
    feature_phase()
    a_out_phase(0, 8)
    wo_prefetch()
    exchange_half(0, ag_outA)
    a_out_phase(8, NH)
    exchange_half(1, ag_outB)

    with nc.Block() as block:
        @block.tensor
        def _(E):
            for f in ops["tensor"]:
                f(E)

        @block.vector
        def _(E):
            for f in ops["vector"]:
                f(E)

        @block.scalar
        def _(E):
            for f in ops["scalar"]:
                f(E)

        @block.sync
        def _(E):
            for f in ops["sync"]:
                f(E)

        @block.gpsimd
        def _(E):
            for f in ops["gpsimd"]:
                f(E)

    emit_agread_switch()

    for k in ops:
        ops[k] = []
        water[k] = {}

    rows_wo_phase()
    v_ln3 = ln_tail()
    f2 = ffn_gateup(v_ln3)
    ffn_down(f2, "x2s", x2s, "ofm", ofm, ln_follow=False)
    wait("sync", "st0", cnt["st0"])
    wait("sync", "st1", cnt["st1"])

    with nc.Block() as block2:
        @block2.tensor
        def _(E):
            for f in ops["tensor"]:
                f(E)

        @block2.vector
        def _(E):
            for f in ops["vector"]:
                f(E)

        @block2.scalar
        def _(E):
            for f in ops["scalar"]:
                f(E)

        @block2.sync
        def _(E):
            for f in ops["sync"]:
                f(E)

        @block2.gpsimd
        def _(E):
            for f in ops["gpsimd"]:
                f(E)

    ctx.close()
    return nc


def _host_prep(inputs):
    x = np.asarray(inputs["L0_attn_hidden_states"])
    Wq = np.ascontiguousarray(np.asarray(inputs["L0_attn_Wq"]).T)
    Wk = np.ascontiguousarray(np.asarray(inputs["L0_attn_Wk"]).T)
    Wv = np.ascontiguousarray(np.asarray(inputs["L0_attn_Wv"]).T)
    Wo = np.ascontiguousarray(np.asarray(inputs["L0_attn_Wo"]).T)
    bq = np.ascontiguousarray(np.asarray(inputs["L0_attn_bq"]).reshape(NH, 128).T)
    bk = np.ascontiguousarray(np.asarray(inputs["L0_attn_bk"]).reshape(NKV, 128).T)
    bv = np.ascontiguousarray(np.asarray(inputs["L0_attn_bv"]).reshape(NKV, 128).T)
    bo = np.ascontiguousarray(np.asarray(inputs["L0_attn_bo"]).reshape(OC, 128).T)
    R = np.asarray(inputs["L0_attn_rand_matrix"])
    gw = np.ascontiguousarray(np.asarray(inputs["L0_ffn_gate_w"]))
    uw = np.ascontiguousarray(np.asarray(inputs["L0_ffn_up_w"]))
    dw = np.ascontiguousarray(np.asarray(inputs["L0_ffn_down_w"]))
    gs = np.ascontiguousarray(np.asarray(inputs["L0_ffn_gate_scale"]).reshape(IC, 128).T)
    us = np.ascontiguousarray(np.asarray(inputs["L0_ffn_up_scale"]).reshape(IC, 128).T)
    ds = np.ascontiguousarray(np.asarray(inputs["L0_ffn_down_scale"]).reshape(OC, 128).T)

    RT = np.ascontiguousarray(R.T)
    R2T = np.ascontiguousarray(np.roll(R, -64, axis=1).T)
    rtdup = np.ascontiguousarray(np.concatenate([RT, RT], axis=1))
    r2tdup = np.ascontiguousarray(np.concatenate([R2T, R2T], axis=1))

    cmean = np.full((128, 128), 1.0 / 2048.0, np.float32)
    cone = np.full((128, 1), 1.0, np.float32)
    caf = np.full((128, 128), 2.0 / 128.0, np.float32)
    crows = np.ones((1, 512), np.float32)
    cshift = np.concatenate([np.zeros(64), np.full(64, math.pi / 2)]).astype(np.float32).reshape(1, 128)
    ceps = np.full((1, 128), 1e-12, np.float32)

    in_maps = []
    for c in range(NCORE):
        xs = np.ascontiguousarray(x[0, c * S:(c + 1) * S, :].T)
        pos = np.arange(c * S, (c + 1) * S, dtype=np.float32)
        invf = (1.0 / (np.float32(THETA) ** (np.arange(0, HD, 2, dtype=np.float32) / np.float32(HD)))).astype(np.float32)
        freqs = pos[None, :] * invf[:, None]
        emb = np.concatenate([freqs, freqs], axis=0)
        cosz = np.cos(emb).astype(np.float32)
        sin_emb = np.sin(emb).astype(np.float32)
        dd = np.arange(HD)[:, None]
        sin2 = np.where(dd < 64, -sin_emb, sin_emb)
        sinp = np.ascontiguousarray(np.roll(sin2, -64, axis=0)).astype(np.float32)
        in_maps.append({
            "xfm": xs, "wqT": Wq, "wkT": Wk, "wvT": Wv, "woT": Wo,
            "gw": gw, "uw": uw, "dw": dw,
            "cosz": cosz, "sinp": sinp,
            "rtdup": rtdup, "r2tdup": r2tdup,
            "bqv": bq, "bkv": bk, "bvv": bv, "bov": bo,
            "gsv": gs, "usv": us, "dsv": ds,
            "cmean": cmean, "cone": cone, "caf": caf,
            "crows": crows, "cshift": cshift, "ceps": ceps,
        })
    return in_maps


def kernel(**inputs):
    in_maps = _host_prep(inputs)
    nc = build_nc()
    res = run_bass_kernel_spmd(nc, in_maps, core_ids=list(range(NCORE)))
    out = np.empty((1, SF, D), np.float32)
    for c in range(NCORE):
        ofm = np.asarray(res.results[c]["ofm"])
        out[0, c * S:(c + 1) * S, :] = ofm.reshape(D, S).T
    return out
